# version 01 (final)
# baseline (speedup 1.0000x reference)
import numpy as np
import ml_dtypes
from contextlib import ExitStack

import concourse.bass as bass
import concourse.mybir as mybir
from concourse.bass_utils import run_bass_kernel_spmd

F32 = mybir.dt.float32
BF16 = mybir.dt.bfloat16
I32 = mybir.dt.int32
I16 = mybir.dt.int16
U8 = mybir.dt.uint8
ALU = mybir.AluOpType
AF = mybir.ActivationFunctionType
AX = mybir.AxisListType

HID = 4096
T = 1024
NCH = 32
NCK = 8
INTER = 11008
NG_GU = 32
NG_DN = 86
PC_GU = 2752
PC_DN = 512

QENG = {"sp": "sync", "sc": "scalar", "gp": "gpsimd"}

TABK = {}
_o = 0
for _n, _L, _a, _c in [
    ("wi", 128, 1, 0), ("wkt", 128, -1, 127),
    ("r16", 16, 1, -8), ("k16", 16, -1, 7),
    ("rl4", 16, 1, 0), ("kl4", 16, -1, 15),
    ("rl5", 32, 1, 0), ("kl5", 32, -1, 31),
    ("rl6", 64, 1, 0), ("kl6", 64, -1, 63),
]:
    TABK[_n] = (_o, _L, _a, _c)
    _o += _L
assert _o == 512


class Builder:
    def __init__(self, nc, io, sb, ps, semh):
        self.nc, self.io, self.sb, self.ps, self.semh = nc, io, sb, ps, semh
        self.E = None
        self.eng = None
        self.v = {k: 0 for k in semh}
        self.marks = {}
        self.p0 = dict(self.v)

    def begin(self, E, eng):
        self.E, self.eng = E, eng
        self.v = {k: 0 for k in self.v}
        self.marks = {}
        self.p0 = dict(self.v)

    def phase(self):
        self.p0 = dict(self.v)

    def on(self, *names):
        return self.E in names

    def w(self, sem, val=None):
        self.eng.wait_ge(self.semh[sem], self.v[sem] if val is None else val)

    def mark(self, key, sem):
        self.marks[key] = self.v[sem]

    def wmk(self, key, sem):
        return max(self.marks.get(key, 0), self.p0[sem])

    def tick(self, sem, ins):
        self.v[sem] += 1
        if ins is not None:
            ins.then_inc(self.semh[sem], 1)

    def selfsync(self, sem, ins):
        self.v[sem] += 1
        if ins is not None:
            ins.then_inc(self.semh[sem], 1)
            self.w(sem)

    def _waits(self, wait):
        out = []
        for s in wait:
            if isinstance(s, tuple):
                out.append((s[0], s[1]))
            else:
                out.append((s, self.v[s]))
        return out

    def dma(self, q, out, in_, wait=(), **kw):
        rw = self._waits(wait)
        self.v[q] += 16
        if self.E == QENG[q]:
            for s, val in rw:
                self.eng.wait_ge(self.semh[s], val)
            self.eng.dma_start(out=out, in_=in_, **kw).then_inc(self.semh[q], 16)

    def dmat(self, q, out, in_, wait=()):
        rw = self._waits(wait)
        self.v[q] += 16
        if self.E == QENG[q]:
            for s, val in rw:
                self.eng.wait_ge(self.semh[s], val)
            self.eng.dma_start_transpose(out=out, in_=in_).then_inc(self.semh[q], 16)


class Arena:
    def __init__(self, nc, st, name, nbytes):
        self.h = st.enter_context(nc.sbuf_tensor(name, [128, nbytes], U8))
        self.base = 0

    def reset(self):
        self.base = 0

    def save(self):
        return self.base

    def load(self, pos):
        self.base = pos

    def take(self, dtype, shape):
        assert shape[0] == 128
        ds = mybir.dt.size(dtype)
        n = 1
        for s in shape[1:]:
            n *= s
        off = (self.base + 15) // 16 * 16
        self.base = off + n * ds
        ap = self.h[:, off:off + n * ds].bitcast(dtype)
        if len(shape) == 2:
            return ap
        dims = " ".join(f"d{i}" for i in range(1, len(shape)))
        kw = {f"d{i}": shape[i] for i in range(1, len(shape))}
        return ap.rearrange(f"p ({dims}) -> p {dims}", **kw)


def build_nc(debug_outs=()):
    nc = bass.Bass("TRN2", target_bir_lowering=False, debug=False)
    io = {}

    def din(name, shape, dt):
        io[name] = nc.dram_tensor(name, shape, dt, kind="ExternalInput").ap()

    din("x", [T + 1, HID], F32)
    for wn in ["wkT", "wvT", "wrT", "wgT", "woT"]:
        din(wn, [HID, HID], F32)
    din("tmix", [128, 4, NCH], F32)
    din("td", [128, NCH], F32)
    din("tf", [128, NCH], F32)
    din("gnw", [HID], F32)
    din("gnb", [HID], F32)
    din("qguT", [PC_GU, HID], I32)
    din("zguT", [PC_GU, NG_GU], I32)
    din("sguT", [2 * INTER, NG_GU], F32)
    din("qdnT", [PC_DN, INTER], I32)
    din("zdnT", [PC_DN, NG_DN], I32)
    din("sdnT", [HID, NG_DN], F32)
    din("qexp", [8, 128], F32)
    io["out"] = nc.dram_tensor("out", [T, HID], F32, kind="ExternalOutput").ap()

    def dint(name, shape, dt):
        kind = "ExternalOutput" if name in debug_outs else "Internal"
        io[name] = nc.dram_tensor(name, shape, dt, kind=kind).ap()

    dint("xn_tok", [1152, HID], BF16)
    for s in ["k_scr", "v_scr", "r_scr", "g_scr", "kt_scr", "o_scr", "s_scr", "ffn_scr"]:
        dint(s, [HID, T], BF16)
    dint("gn_scr", [T, HID], BF16)
    dint("wgu_scr", [2 * INTER, HID], BF16)
    dint("wdn_scr", [HID, INTER], BF16)
    dint("h_scr", [INTER, T], BF16)
    dint("cc_in", [128, NCH * 64], F32)
    dint("cc_out", [1024, NCH * 64], F32)

    with ExitStack() as st:
        e = st.enter_context
        semh = {}
        for n in ["sp", "sc", "gp", "ve", "ac", "te", "cc"]:
            semh[n] = e(nc.semaphore("sem_" + n))

        sb = {}

        def sbuf(name, shape, dt):
            sb[name] = e(nc.sbuf_tensor(name, shape, dt))

        sbuf("et", [128, NCH], F32)
        sbuf("ut", [128, NCH], F32)
        sbuf("tmx", [128, 4, NCH], F32)
        sbuf("wc", [128, NCH], F32)
        sbuf("w128n", [128, NCH, NCK], F32)
        sbuf("m16", [128, 128], U8)
        sbuf("idm", [128, 128], U8)
        sbuf("ioti", [128, 128], I32)
        sbuf("iotj", [128, 128], I32)
        sbuf("itmp", [128, 128], I32)
        sbuf("itmp2", [128, 128], I32)
        sbuf("fiot", [128, 128], F32)
        sbuf("ftmp", [128, 512], F32)
        sbuf("sfac", [128, NCH], F32)
        sbuf("qxp", [128, 8], F32)
        sbuf("stat", [128, 16], F32)
        sbuf("eps", [128, 1], F32)
        sbuf("flush", [1, 64], F32)
        sbuf("zb", [128, HID], BF16)
        sbuf("e2", [128, 2], BF16)
        sbuf("ones", [1, 128], BF16)
        sbuf("prow", [1, HID], BF16)
        sbuf("zrow", [1, 128], BF16)
        sbuf("ev_", [128, 4, 512], BF16)
        sb["ev"] = [sb["ev_"][:, i, :] for i in range(4)]
        sbuf("ev2_", [128, 4, 512], BF16)
        sb["ev2"] = [sb["ev2_"][:, i, :] for i in range(4)]
        sbuf("wstr_", [128, 4, 512], BF16)
        sb["wstr"] = [sb["wstr_"][:, i, :] for i in range(4)]
        sbuf("osb", [128, 4, 128], BF16)
        sbuf("asb_", [128, 2, 128], BF16)
        sb["asb"] = [sb["asb_"][:, i, :] for i in range(2)]
        sbuf("gst", [128, 64, 6], F32)

        ar = Arena(nc, st, "arena", 156 * 1024)
        sb["dq_q"] = [ar.take(I32, [128, 1024]) for _ in range(2)]
        sb["dq_e"] = ar.take(I16, [128, 4, 2048])
        sb["dq_w"] = [ar.take(BF16, [128, 8, 1024]) for _ in range(2)]
        sb["dq_s"] = ar.take(F32, [128, 8, NG_GU])
        sb["dq_z"] = ar.take(F32, [128, 8, NG_GU])
        sb["dq_zq"] = ar.take(I32, [128, NG_GU])
        sb["dn_s"] = ar.take(F32, [128, 8, NG_DN])
        sb["dn_z"] = ar.take(F32, [128, 8, NG_DN])
        sb["dn_zq"] = ar.take(I32, [128, NG_DN])
        ar.reset()
        sb["ln_x"] = [ar.take(F32, [128, HID]) for _ in range(2)]
        sb["ln_o"] = [ar.take(BF16, [128, HID]) for _ in range(2)]
        sb["ln_sq"] = ar.take(F32, [128, HID])
        ar.reset()
        sb["mix"] = ar.take(BF16, [128, NCH, T])
        sb["wmac"] = [ar.take(BF16, [128, NCH, 512]) for _ in range(2)]
        sb["xnf"] = [ar.take(BF16, [128, 1152]) for _ in range(2)]
        ar.reset()
        sb["tabs"] = ar.take(BF16, [128, NCH, 512])
        sb["vtok"] = [ar.take(BF16, [128, HID]) for _ in range(4)]
        sb["kttok"] = [ar.take(BF16, [128, HID]) for _ in range(4)]
        sb["snap"] = ar.take(BF16, [128, NCK, NCH, 64])
        sb["sin"] = ar.take(F32, [128, NCH, 64])
        u0 = ar.save()
        sb["dec"] = ar.take(BF16, [128, 9, 512])
        sb["kf"] = ar.take(BF16, [128, T])
        sb["rf"] = ar.take(BF16, [128, T])
        sb["rku"] = ar.take(BF16, [128, 512])
        u1 = ar.save()
        ar.load(u0)
        sb["sres"] = ar.take(F32, [128, NCH, 64])
        sb["ccs"] = ar.take(F32, [128, NCH, 64])
        u2 = ar.save()
        ar.load(u0)
        sb["arg"] = ar.take(F32, [128, NCH, 128])
        u3 = ar.save()
        ar.load(max(u1, u2, u3))
        ar.reset()
        sb["gtile"] = [ar.take(BF16, [128, HID]) for _ in range(2)]
        sb["gsq"] = ar.take(F32, [128, HID])
        sb["xt"] = [ar.take(F32, [128, HID]) for _ in range(2)]
        sb["stok"] = ar.take(BF16, [128, HID])
        sb["gnwt"] = ar.take(BF16, [128, HID])
        sb["gnbt"] = ar.take(BF16, [128, HID])
        ar.reset()
        sb["hbuf"] = ar.take(BF16, [128, NG_DN, 512])

        ps = [e(nc.psum_tensor(f"pb{i}", [128, 512], F32)) for i in range(8)]

        blk = e(nc.Block())
        b = Builder(nc, io, sb, ps, semh)

        finals = {}

        def emit(E):
            def f(eng):
                b.begin(E, eng)
                program(b)
                finals[E] = dict(b.v)
                if len(finals) > 1:
                    vals = list(finals.values())
                    assert all(x == vals[0] for x in vals), \
                        f"ledger drift: {finals}"
            return f

        blk.sync(emit("sync"))
        blk.scalar(emit("scalar"))
        blk.vector(emit("vector"))
        blk.tensor(emit("tensor"))
        blk.gpsimd(emit("gpsimd"))

    return nc


def program(b):
    ph_consts(b)
    ph_dequant(b, "gu")
    ph_dequant(b, "dn")
    ph_ln(b)
    ph_mix_proj(b, "wkT", "k_scr", 0)
    ph_mix_proj(b, "wvT", "v_scr", 1)
    ph_mix_proj(b, "wrT", "r_scr", 2)
    ph_mix_proj(b, "wgT", "g_scr", 3)
    ph_tables(b)
    ph_ktdec(b)
    ph_passA(b)
    ph_passB(b)
    ph_gn(b)
    ph_gate(b)
    ph_matmul(b, "woT", "s_scr")
    ph_gateup(b)
    ph_down(b)
    ph_final(b)
    if b.on("sync"):
        b.w("sp")
        b.w("sc")
        b.w("gp")


def ph_consts(b):
    io, sb = b.io, b.sb
    b.phase()
    b.dma("sp", sb["et"][:, :], io["td"])
    b.dma("sp", sb["ut"][:, :], io["tf"])
    b.dma("sp", sb["tmx"][:, :, :], io["tmix"])
    for r in range(8):
        b.dma("sp", sb["qxp"][:, r:r + 1], io["qexp"][r].rearrange("(p o) -> p o", o=1))

    ins = None
    if b.on("gpsimd"):
        b.w("sp")
        g = b.eng
        g.iota(sb["iotj"][:, :], pattern=[[0, 128]], base=0, channel_multiplier=1)
        g.iota(sb["ioti"][:, :], pattern=[[1, 128]], base=0, channel_multiplier=0)
        ins = g.iota(sb["fiot"][:, :], pattern=[[1, 128]], base=0,
                     channel_multiplier=0, allow_small_or_imprecise_dtypes=True)
    b.tick("gp", ins)

    if b.on("vector"):
        b.w("gp")
    v = b.eng if b.on("vector") else None
    b.selfsync("ve", v.tensor_scalar(out=sb["itmp"][:, :], in0=sb["iotj"][:, :],
               scalar1=4, scalar2=None, op0=ALU.logical_shift_right) if v else None)
    b.selfsync("ve", v.tensor_scalar(out=sb["itmp2"][:, :], in0=sb["ioti"][:, :],
               scalar1=4, scalar2=None, op0=ALU.logical_shift_right) if v else None)
    b.selfsync("ve", v.tensor_tensor(out=sb["itmp"][:, :], in0=sb["itmp"][:, :],
               in1=sb["itmp2"][:, :], op=ALU.is_equal) if v else None)
    b.selfsync("ve", v.tensor_tensor(out=sb["itmp2"][:, :], in0=sb["ioti"][:, :],
               in1=sb["iotj"][:, :], op=ALU.is_gt) if v else None)
    b.selfsync("ve", v.tensor_tensor(out=sb["itmp"][:, :], in0=sb["itmp"][:, :],
               in1=sb["itmp2"][:, :], op=ALU.logical_and) if v else None)
    b.selfsync("ve", v.tensor_copy(out=sb["m16"][:, :], in_=sb["itmp"][:, :]) if v else None)
    b.selfsync("ve", v.tensor_tensor(out=sb["itmp"][:, :], in0=sb["ioti"][:, :],
               in1=sb["iotj"][:, :], op=ALU.is_equal) if v else None)
    b.selfsync("ve", v.tensor_copy(out=sb["idm"][:, :], in_=sb["itmp"][:, :]) if v else None)
    ins = None
    if b.on("vector"):
        v.memset(sb["zrow"][:, :], 0.0)
        v.memset(sb["zb"][:, :], 0.0)
        v.memset(sb["e2"][:, :], 0.0)
        v.memset(sb["e2"][0:64, 0:1], 1.0)
        v.memset(sb["e2"][64:128, 1:2], 1.0)
        v.memset(sb["ones"][:, :], 1.0)
        ins = v.memset(sb["eps"][:, :], 1e-5)
    b.tick("ve", ins)

    ins = None
    if b.on("scalar"):
        b.w("sp")
        b.w("ve")
        ins = b.eng.activation(out=sb["et"][:, :], in_=sb["et"][:, :], func=AF.Exp)
    b.tick("ac", ins)


def ph_dequant(b, which):
    io, sb = b.io, b.sb
    b.phase()
    if which == "gu":
        qsrc, zsrc, ssrc, wdst = io["qguT"], io["zguT"], io["sguT"], io["wgu_scr"]
        blocks = [(i * 128, 128) for i in range(21)] + [(2688, 64)]
        KTOT = HID
        s_t, z_t, zq_t = sb["dq_s"], sb["dq_z"], sb["dq_zq"]
    else:
        qsrc, zsrc, ssrc, wdst = io["qdnT"], io["zdnT"], io["sdnT"], io["wdn_scr"]
        blocks = [(i * 128, 128) for i in range(4)]
        KTOT = INTER
        s_t, z_t, zq_t = sb["dn_s"], sb["dn_z"], sb["dn_zq"]

    quarters = []
    k0 = 0
    while k0 < KTOT:
        kk = min(1024, KTOT - k0)
        quarters.append((k0, kk))
        k0 += kk

    qg = 0
    for c0, P in blocks:
        b.dma("sc", s_t[0:P, :, :],
              ssrc[8 * c0:8 * c0 + 8 * P, :].rearrange("(p j) g -> p j g", j=8),
              wait=["ve", "ac"])
        b.dma("sc", zq_t[0:P, :], zsrc[c0:c0 + P, :])
        ins = None
        if b.on("vector"):
            b.w("sc")
            b.w("ac")
            v = b.eng
            for j in range(8):
                v.tensor_scalar(out=z_t[0:P, j, :].bitcast(I32),
                                in0=zq_t[0:P, :], scalar1=4 * j, scalar2=15,
                                op0=ALU.logical_shift_right, op1=ALU.bitwise_and)
            v.tensor_copy(out=z_t[0:P, :, :], in_=z_t[0:P, :, :].bitcast(I32))
            ins = v.scalar_tensor_tensor(out=z_t[0:P, :, :], in0=z_t[0:P, :, :],
                                         scalar=-1.0, in1=s_t[0:P, :, :],
                                         op0=ALU.mult, op1=ALU.mult)
        b.tick("ve", ins)

        for k0, kk in quarters:
            qb = qg % 2
            b.dma("sc", sb["dq_q"][qb][0:P, 0:kk], qsrc[c0:c0 + P, k0:k0 + kk],
                  wait=["ve"])
            q16 = sb["dq_q"][qb][0:P, 0:kk].bitcast(I16)
            ins = None
            if b.on("vector"):
                b.w("sc")
                b.w("ac")
                for s4 in range(4):
                    ins = b.eng.tensor_scalar(
                        out=sb["dq_e"][0:P, s4, 0:2 * kk], in0=q16,
                        scalar1=4 * s4, scalar2=15,
                        op0=ALU.logical_shift_right, op1=ALU.bitwise_and)
            b.tick("ve", ins)

            g_lo, g_hi = k0 // 128, (k0 + kk) // 128
            wb = qg % 2
            ins = None
            if b.on("scalar"):
                b.w("ve")
                b.w("sp", b.wmk(("dqst", which, qg - 2), "sp"))
                for j in range(8):
                    s4, hb = j % 4, j // 4
                    ein = sb["dq_e"][0:P, s4, 0:2 * kk].rearrange(
                        "p (k h) -> p h k", h=2)
                    for g in range(g_lo, g_hi):
                        kin = g * 128 - k0
                        ins = b.eng.activation(
                            out=sb["dq_w"][wb][0:P, j, kin:kin + 128],
                            in_=ein[:, hb, kin:kin + 128],
                            func=AF.Identity,
                            scale=s_t[0:P, j, g:g + 1],
                            bias=z_t[0:P, j, g:g + 1])
            b.tick("ac", ins)
            for j in range(8):
                b.dma("sp",
                      wdst.rearrange("(r e) k -> r e k", e=8)[c0:c0 + P, j, k0:k0 + kk],
                      sb["dq_w"][wb][0:P, j, 0:kk], wait=["ac"])
            b.mark(("dqst", which, qg), "sp")
            qg += 1


def ph_ln(b):
    io, sb = b.io, b.sb
    b.phase()
    inv = 1.0 / HID
    for ti in range(9):
        r0, nr = (1 + 128 * ti, 128) if ti < 8 else (0, 1)
        bf = ti % 2
        b.dma("sp", sb["ln_x"][bf][0:nr, :], io["x"][r0:r0 + nr, :],
              wait=["ve", "ac", "sp"])
        ins = None
        if b.on("vector"):
            b.w("sp")
            ins = b.eng.reduce_sum(sb["stat"][0:nr, 0:1], sb["ln_x"][bf][0:nr, :], AX.X)
        b.tick("ve", ins)
        ins = None
        if b.on("scalar"):
            b.w("ve")
            ins = b.eng.activation(out=sb["ln_sq"][0:nr, :], in_=sb["ln_x"][bf][0:nr, :],
                                   func=AF.Square, accum_out=sb["stat"][0:nr, 1:2])
        b.tick("ac", ins)
        if b.on("vector"):
            b.w("ac")
        v = b.eng if b.on("vector") else None
        b.selfsync("ve", v.tensor_scalar(out=sb["stat"][0:nr, 2:3],
                   in0=sb["stat"][0:nr, 0:1], scalar1=inv, scalar2=None,
                   op0=ALU.mult) if v else None)
        b.selfsync("ve", v.tensor_tensor(out=sb["stat"][0:nr, 3:4],
                   in0=sb["stat"][0:nr, 2:3], in1=sb["stat"][0:nr, 2:3],
                   op=ALU.mult) if v else None)
        b.selfsync("ve", v.scalar_tensor_tensor(out=sb["stat"][0:nr, 4:5],
                   in0=sb["stat"][0:nr, 1:2], scalar=inv,
                   in1=sb["stat"][0:nr, 3:4], op0=ALU.mult,
                   op1=ALU.subtract) if v else None)
        s = b.eng if b.on("scalar") else None
        if s:
            b.w("ve")
        b.selfsync("ac", s.activation(out=sb["stat"][0:nr, 5:6],
                   in_=sb["stat"][0:nr, 4:5], func=AF.Ln,
                   bias=sb["eps"][0:nr, 0:1]) if s else None)
        b.selfsync("ac", s.activation(out=sb["stat"][0:nr, 6:7],
                   in_=sb["stat"][0:nr, 5:6], func=AF.Exp, scale=-0.5) if s else None)
        ins = None
        if b.on("vector"):
            b.w("ac")
            ins = b.eng.tensor_scalar(out=sb["ln_o"][bf][0:nr, :],
                                      in0=sb["ln_x"][bf][0:nr, :],
                                      scalar1=sb["stat"][0:nr, 2:3],
                                      scalar2=sb["stat"][0:nr, 6:7],
                                      op0=ALU.subtract, op1=ALU.mult)
        b.tick("ve", ins)
        b.dma("sc", io["xn_tok"][r0:r0 + nr, :], sb["ln_o"][bf][0:nr, :], wait=["ve"])
    b.dma("sc", io["xn_tok"][1025:1152, :], sb["zb"][0:127, :], wait=["ve"])


def ph_mix_proj(b, wname, dst, mix_i):
    io, sb = b.io, b.sb
    b.phase()
    for cc in range(NCH):
        bf = cc % 2
        b.dmat("sc", sb["xnf"][bf][:, :],
               io["xn_tok"].rearrange("t (c p) -> t c p", p=128)[:, cc, :],
               wait=["ve", "ac", "sc"])
        ins = None
        if b.on("vector"):
            if cc == 0:
                b.w("te")
                b.w("ac")
            b.w("sc")
            xn = sb["xnf"][bf]
            v = b.eng
            for hf in range(2):
                sl = slice(512 * hf, 512 * hf + 512)
                v.tensor_tensor(out=sb["ev"][2 + hf][:, :],
                                in0=xn[:, 1 + 512 * hf:513 + 512 * hf],
                                in1=xn[:, 512 * hf:512 * hf + 512], op=ALU.subtract)
                ins = v.scalar_tensor_tensor(
                    out=sb["mix"][:, cc, sl], in0=sb["ev"][2 + hf][:, :],
                    scalar=sb["tmx"][:, mix_i, cc:cc + 1],
                    in1=xn[:, 512 * hf:512 * hf + 512], op0=ALU.mult, op1=ALU.add)
        b.tick("ve", ins)
    ph_matmul(b, wname, dst)


def ph_matmul(b, wname, dst, lowbanks=False):
    io, sb = b.io, b.sb
    b.phase()
    for nm in range(8):
        wb = nm % 2
        for pc in range(8):
            b.dma("gp", sb["wmac"][wb][:, 4 * pc:4 * pc + 4, :],
                  io[wname].rearrange("(c p) n -> p c n", p=128)[
                      :, 4 * pc:4 * pc + 4, 512 * nm:512 * nm + 512],
                  wait=[("te", b.wmk(("pm_wm", nm - 2), "te")),
                        ("sc", b.p0["sc"]), ("sp", b.p0["sp"])] if pc == 0 else [])
        for th in range(2):
            pb = [4 * th + i for i in range(4)]
            if lowbanks:
                pb = [i for i in range(4)]
            ins = None
            if b.on("tensor"):
                b.w("gp")
                b.w("ve")
                if lowbanks:
                    b.w("ac", b.wmk(("pm_evg", 2 * nm + th - 1), "ac"))
                else:
                    b.w("ac", b.wmk(("pm_ev", nm - 1, th), "ac"))
                for dc in range(NCH):
                    for ic in range(4):
                        ins = b.eng.matmul(
                            b.ps[pb[ic]][:, :],
                            sb["wmac"][wb][:, dc, 128 * ic:128 * ic + 128],
                            sb["mix"][:, dc, 512 * th:512 * th + 512],
                            start=(dc == 0), stop=(dc == NCH - 1))
            b.tick("te", ins)
            b.mark(("pm_wm", nm), "te")
            ins = None
            if b.on("scalar"):
                b.w("te")
                b.w("sc", b.wmk(("pm_st", nm, th, "prev"), "sc"))
                for ic in range(4):
                    ins = b.eng.activation(out=sb["ev"][ic][:, :], in_=b.ps[pb[ic]][:, :],
                                           func=AF.Identity)
            b.tick("ac", ins)
            b.mark(("pm_ev", nm, th), "ac")
            b.mark(("pm_evg", 2 * nm + th), "ac")
            for ic in range(4):
                n0 = 512 * nm + 128 * ic
                b.dma("sc", io[dst][n0:n0 + 128, 512 * th:512 * th + 512],
                      sb["ev"][ic][:, :], wait=["ac"])
            nxt = ("pm_st", nm, th + 1, "prev") if th == 0 else ("pm_st", nm + 1, 0, "prev")
            b.mark(nxt, "sc")


def ph_tables(b):
    io, sb = b.io, b.sb
    b.phase()
    for kind, (off, L, a, c) in TABK.items():
        if b.on("vector"):
            b.w("ac")
            if kind == "wi":
                b.w("te")
        v = b.eng if b.on("vector") else None
        b.selfsync("ve", v.tensor_scalar(out=sb["ftmp"][:, 0:L],
                   in0=sb["fiot"][:, 0:L], scalar1=float(-a), scalar2=float(-c),
                   op0=ALU.mult, op1=ALU.add) if v else None)
        ins = None
        if b.on("vector"):
            ins = v.tensor_tensor(
                out=sb["arg"][:, :, 0:L],
                in0=sb["ftmp"][:, None, 0:L].broadcast_to([128, NCH, L]),
                in1=sb["et"][:, :, None].broadcast_to([128, NCH, L]),
                op=ALU.mult)
        b.tick("ve", ins)
        ins = None
        if b.on("scalar"):
            b.w("ve")
            if kind == "wi":
                b.w("te")
            ins = b.eng.activation(out=sb["tabs"][:, :, off:off + L],
                                   in_=sb["arg"][:, :, 0:L], func=AF.Exp)
        b.tick("ac", ins)
    ins = None
    if b.on("scalar"):
        s = b.eng
        s.activation(out=sb["wc"][:, :], in_=sb["et"][:, :], func=AF.Exp, scale=-128.0)
        for n in range(NCK):
            ins = s.activation(out=sb["w128n"][:, :, n], in_=sb["et"][:, :],
                               func=AF.Exp, scale=float(-128.0 * n))
    b.tick("ac", ins)


def ph_ktdec(b):
    io, sb = b.io, b.sb
    b.phase()
    off, L, _, _ = TABK["wkt"]
    for cc in range(NCH):
        b.dma("sp", sb["kf"][:, :], io["k_scr"][128 * cc:128 * cc + 128, :],
              wait=["sc", "ve", "te"])
        ins = None
        if b.on("vector"):
            b.w("sp")
            b.w("ac")
            ins = b.eng.tensor_tensor(
                out=sb["rf"][:, :].rearrange("p (c l) -> p c l", l=128),
                in0=sb["kf"][:, :].rearrange("p (c l) -> p c l", l=128),
                in1=sb["tabs"][:, cc, None, off:off + L].broadcast_to([128, NCK, 128]),
                op=ALU.mult)
        b.tick("ve", ins)
        b.dma("sc", io["kt_scr"][128 * cc:128 * cc + 128, :], sb["rf"][:, :],
              wait=["ve"])


def ph_passA(b):
    io, sb = b.io, b.sb
    b.phase()
    ins = None
    if b.on("vector"):
        b.w("te")
        b.w("ac")
        b.eng.memset(sb["sres"][:, :, :], 0.0)
        ins = b.eng.memset(sb["sin"][:, :, :], 0.0)
    b.tick("ve", ins)
    for ch in range(NCK):
        vb = ch % 2
        b.dmat("sc", sb["vtok"][vb][:, :], io["v_scr"][:, 128 * ch:128 * ch + 128],
               wait=["sc", ("te", b.wmk(("pA", ch - 2), "te"))])
        b.dmat("sc", sb["kttok"][vb][:, :], io["kt_scr"][:, 128 * ch:128 * ch + 128],
               wait=[("te", b.wmk(("pA", ch - 2), "te"))])
        for cc in range(NCH):
            po6 = 64 * (cc % 2)
            v = b.eng if b.on("vector") else None
            if v and cc == 0:
                b.w("sp")
                b.w("sc")
            b.selfsync("ve", v.tensor_copy(out=sb["snap"][:, ch, cc, :],
                       in_=sb["sres"][:, cc, :]) if v else None)
            ins = None
            if b.on("tensor"):
                if cc == 0:
                    b.w("sp")
                    b.w("sc")
                b.w("ve")
                for hh in range(2):
                    fs = 128 * cc + 64 * hh
                    ins = b.eng.matmul(
                        b.ps[6][64 * hh:64 * hh + 64, po6:po6 + 64],
                        sb["kttok"][vb][:, fs:fs + 64],
                        sb["vtok"][vb][:, fs:fs + 64],
                        start=True, stop=True, skip_group_check=True,
                        tile_position=(0, 64 * hh))
            b.tick("te", ins)
            if v:
                b.w("te")
            b.selfsync("ve", v.scalar_tensor_tensor(
                out=sb["sres"][:, cc, :], in0=sb["sres"][:, cc, :],
                scalar=sb["wc"][:, cc:cc + 1], in1=b.ps[6][:, po6:po6 + 64],
                op0=ALU.mult, op1=ALU.add) if v else None)
        b.mark(("pA", ch), "te")
    b.dma("gp", io["cc_in"].rearrange("p (c e) -> p c e", e=64), sb["sres"][:, :, :],
          wait=["ve"])
    if b.on("gpsimd"):
        b.w("gp")
        b.eng.collective_compute(
            "AllGather", ALU.bypass, replica_groups=[[0, 1, 2, 3, 4, 5, 6, 7]],
            ins=[io["cc_in"]], outs=[io["cc_out"]]).then_inc(b.semh["cc"], 1)
    b.v["cc"] += 1
    for r in range(8):
        b.dma("gp", sb["ccs"][:, :, :],
              io["cc_out"].rearrange("(r p) (c e) -> r p c e", p=128, e=64)[r],
              wait=["cc", "ve"])
        ins = None
        if b.on("scalar"):
            b.w("ve")
            ins = b.eng.activation(out=sb["sfac"][:, :], in_=sb["et"][:, :],
                                   func=AF.Exp, scale=sb["qxp"][:, r:r + 1])
        b.tick("ac", ins)
        ins = None
        if b.on("vector"):
            b.w("gp")
            b.w("ac")
            v = b.eng
            v.tensor_tensor(out=sb["ccs"][:, :, :], in0=sb["ccs"][:, :, :],
                            in1=sb["sfac"][:, :, None].broadcast_to([128, NCH, 64]),
                            op=ALU.mult)
            ins = v.tensor_tensor(out=sb["sin"][:, :, :], in0=sb["sin"][:, :, :],
                                  in1=sb["ccs"][:, :, :], op=ALU.add)
        b.tick("ve", ins)
    for ch in range(NCK):
        ins = None
        if b.on("vector"):
            for cc in range(NCH):
                ins = b.eng.scalar_tensor_tensor(
                    out=sb["snap"][:, ch, cc, :], in0=sb["sin"][:, cc, :],
                    scalar=sb["w128n"][:, cc, ch:ch + 1],
                    in1=sb["snap"][:, ch, cc, :], op0=ALU.mult, op1=ALU.add)
        b.tick("ve", ins)


def ph_passB(b):
    io, sb = b.io, b.sb
    b.phase()
    for half in range(2):
        ch0 = 4 * half
        for q in range(4):
            ch = ch0 + q
            b.dmat("sc", sb["vtok"][q][:, :], io["v_scr"][:, 128 * ch:128 * ch + 128],
                   wait=[("te", b.wmk(("pB", half - 1), "te"))])
            b.dmat("sc", sb["kttok"][q][:, :], io["kt_scr"][:, 128 * ch:128 * ch + 128],
                   wait=[("te", b.wmk(("pB", half - 1), "te"))])
        for cc in range(NCH):
            b.dma("sp", sb["kf"][:, 0:512],
                  io["k_scr"][128 * cc:128 * cc + 128, 512 * half:512 * half + 512],
                  wait=["ve"])
            b.dma("sp", sb["rf"][:, 0:512],
                  io["r_scr"][128 * cc:128 * cc + 128, 512 * half:512 * half + 512])
            ins = None
            if b.on("vector"):
                b.w("sp")
                b.w("te", b.wmk(("pBdec", half, cc - 1), "te"))
                v = b.eng
                for ki, kind in enumerate(["wi", "r16", "rl4", "rl5", "rl6"]):
                    off, L, _, _ = TABK[kind]
                    v.tensor_tensor(
                        out=sb["dec"][:, ki, :].rearrange("p (c l) -> p c l", l=L),
                        in0=sb["rf"][:, 0:512].rearrange("p (c l) -> p c l", l=L),
                        in1=sb["tabs"][:, cc, None, off:off + L].broadcast_to(
                            [128, 512 // L, L]),
                        op=ALU.mult)
                for ki, kind in enumerate(["k16", "kl4", "kl5", "kl6"]):
                    off, L, _, _ = TABK[kind]
                    v.tensor_tensor(
                        out=sb["dec"][:, 5 + ki, :].rearrange("p (c l) -> p c l", l=L),
                        in0=sb["kf"][:, 0:512].rearrange("p (c l) -> p c l", l=L),
                        in1=sb["tabs"][:, cc, None, off:off + L].broadcast_to(
                            [128, 512 // L, L]),
                        op=ALU.mult)
                v.tensor_tensor(out=sb["rku"][:, :], in0=sb["rf"][:, 0:512],
                                in1=sb["kf"][:, 0:512], op=ALU.mult)
                ins = v.tensor_scalar(out=sb["rku"][:, :], in0=sb["rku"][:, :],
                                      scalar1=sb["ut"][:, cc:cc + 1], scalar2=None,
                                      op0=ALU.mult)
            b.tick("ve", ins)

            for q in range(4):
                ch = ch0 + q
                po = 256 * (q % 2)
                po4 = 128 * (q % 2)
                po5 = 2 * (q % 2)
                dec = sb["dec"]
                ins = None
                if b.on("tensor"):
                    b.w("ve")
                    b.w("ac", b.wmk(("pBev", half, cc, q - 2), "ac"))
                    b.w("ve", b.wmk(("pBpr", half, cc, q - 2), "ve"))
                    t = b.eng
                    for hh in range(2):
                        hs = slice(64 * hh, 64 * hh + 64)
                        ro = 128 * q

                        def ds(ki, lo, ln):
                            return dec[hs, ki, ro + lo:ro + lo + ln]

                        pa = b.ps[2][:, po + 128 * hh:po + 128 * hh + 128]
                        t.matmul(pa, sb["zrow"][:, :], sb["zrow"][:, :],
                                 start=True, stop=False, skip_group_check=True)
                        t.matmul(pa[0:64, 64:128], ds(8, 0, 64), ds(4, 64, 64),
                                 start=False, stop=False, skip_group_check=True,
                                 tile_position=(64 * hh, 0))
                        for q5 in range(2):
                            t.matmul(pa[64 * q5:64 * q5 + 32,
                                        64 * q5 + 32:64 * q5 + 64],
                                     ds(7, 64 * q5, 32), ds(3, 64 * q5 + 32, 32),
                                     start=False, stop=False, skip_group_check=True,
                                     tile_position=(64 * hh, 64 * q5))
                        for q4 in range(4):
                            t.matmul(pa[32 * q4:32 * q4 + 16,
                                        32 * q4 + 16:32 * q4 + 32],
                                     ds(6, 32 * q4, 16), ds(2, 32 * q4 + 16, 16),
                                     start=False, stop=(q4 == 3), skip_group_check=True,
                                     tile_position=(64 * hh, 32 * q4))
                        t.matmul(b.ps[3][:, po + 128 * hh:po + 128 * hh + 128],
                                 ds(5, 0, 128), ds(1, 0, 128),
                                 start=True, stop=True, skip_group_check=True,
                                 tile_position=(64 * hh, 0))
                    ins = t.matmul(b.ps[5][:, po5:po5 + 2],
                                   sb["rku"][:, 128 * q:128 * q + 128],
                                   sb["e2"][:, :], start=True, stop=True,
                                   skip_group_check=True)
                b.tick("te", ins)
                ins = None
                if b.on("scalar"):
                    b.w("te")
                    b.w("sp", b.wmk(("pBost", half, cc, "prev"), "sp"))
                    for hh in range(2):
                        ins = b.eng.activation(
                            out=sb["asb"][hh][:, :],
                            in_=b.ps[2][:, po + 128 * hh:po + 128 * hh + 128],
                            func=AF.Identity)
                b.tick("ac", ins)
                ins = None
                if b.on("vector"):
                    b.w("te")
                    b.w("ac")
                    v = b.eng
                    for hh in range(2):
                        v.copy_predicated(out=sb["asb"][hh][:, :], mask=sb["m16"][:, :],
                                          data=b.ps[3][:, po + 128 * hh:po + 128 * hh + 128])
                        ins = v.copy_predicated(
                            out=sb["asb"][hh][:, :], mask=sb["idm"][:, :],
                            data=b.ps[5][:, po5 + hh:po5 + hh + 1].broadcast_to([128, 128]))
                b.tick("ve", ins)
                b.mark(("pBpr", half, cc, q), "ve")
                ins = None
                if b.on("tensor"):
                    b.w("ve")
                    t = b.eng
                    for hh in range(2):
                        hs = slice(64 * hh, 64 * hh + 64)
                        t.matmul(b.ps[4][64 * hh:64 * hh + 64, po4:po4 + 128],
                                 sb["snap"][hs, ch, cc, :],
                                 sb["dec"][hs, 0, 128 * q:128 * q + 128],
                                 start=True, stop=False, skip_group_check=True,
                                 tile_position=(64 * hh, 64 * hh))
                        ins = t.matmul(
                            b.ps[4][64 * hh:64 * hh + 64, po4:po4 + 128],
                            sb["vtok"][q][:, 128 * cc + 64 * hh:128 * cc + 64 * hh + 64],
                            sb["asb"][hh][:, :],
                            start=False, stop=True, skip_group_check=True,
                            tile_position=(0, 64 * hh))
                b.tick("te", ins)
                ins = None
                if b.on("scalar"):
                    b.w("te")
                    ins = b.eng.activation(out=sb["osb"][:, q, :],
                                           in_=b.ps[4][:, po4:po4 + 128],
                                           func=AF.Identity)
                b.tick("ac", ins)
                b.mark(("pBev", half, cc, q), "ac")
            b.mark(("pBdec", half, cc), "te")
            b.dma("sp", io["o_scr"][128 * cc:128 * cc + 128, 512 * half:512 * half + 512],
                  sb["osb"][:, :, :].rearrange("p q l -> p (q l)"), wait=["ac"])
            b.mark(("pBost", half, cc + 1, "prev") if cc + 1 < NCH else
                   ("pBost", half + 1, 0, "prev"), "sp")
        b.mark(("pB", half), "te")


def ph_gn(b):
    io, sb = b.io, b.sb
    b.phase()
    i64 = 1.0 / 64
    for which, srcn in (("gnwt", "gnw"), ("gnbt", "gnb")):
        b.dma("gp", sb["prow"][0:1, :], io[srcn][None, :],
              wait=[("ac", b.v["ac"]), ("te", b.v["te"]), ("ve", b.v["ve"])])
        b.dma("gp", sb["flush"][0:1, :], io["gnw"][None, 0:64])
        for sg in range(8):
            ins = None
            if b.on("tensor"):
                b.w("gp")
                b.w("ac", b.v["ac"])
                ins = b.eng.matmul(b.ps[sg % 2][:, :], sb["ones"][:, :],
                                   sb["prow"][0:1, 512 * sg:512 * sg + 512],
                                   start=True, stop=True, skip_group_check=True)
            b.tick("te", ins)
            ins = None
            if b.on("scalar"):
                b.w("te")
                ins = b.eng.activation(out=sb[which][:, 512 * sg:512 * sg + 512],
                                       in_=b.ps[sg % 2][:, :], func=AF.Identity)
            b.tick("ac", ins)
    for tc in range(NCK):
        bf = tc % 2
        b.dmat("sc", sb["gtile"][bf][:, :], io["o_scr"][:, 128 * tc:128 * tc + 128],
               wait=["sp", "ve", "te", "ac"])
        ins = None
        if b.on("vector"):
            b.w("sc")
            ins = b.eng.reduce_sum(
                sb["gst"][:, :, 0:1],
                sb["gtile"][bf][:, :].rearrange("p (h e) -> p h e", e=64), AX.X)
        b.tick("ve", ins)
        ins = None
        if b.on("scalar"):
            b.w("ve")
            ins = b.eng.activation(out=sb["gsq"][:, :], in_=sb["gtile"][bf][:, :],
                                   func=AF.Square)
        b.tick("ac", ins)
        if b.on("vector"):
            b.w("ac")
        v = b.eng if b.on("vector") else None
        b.selfsync("ve", v.reduce_sum(sb["gst"][:, :, 1:2],
                   sb["gsq"][:, :].rearrange("p (h e) -> p h e", e=64),
                   AX.X) if v else None)
        b.selfsync("ve", v.tensor_scalar(out=sb["gst"][:, :, 2:3],
                   in0=sb["gst"][:, :, 0:1], scalar1=i64, scalar2=None,
                   op0=ALU.mult) if v else None)
        b.selfsync("ve", v.tensor_tensor(out=sb["gst"][:, :, 3:4],
                   in0=sb["gst"][:, :, 2:3], in1=sb["gst"][:, :, 2:3],
                   op=ALU.mult) if v else None)
        b.selfsync("ve", v.scalar_tensor_tensor(out=sb["gst"][:, :, 4:5],
                   in0=sb["gst"][:, :, 1:2], scalar=i64,
                   in1=sb["gst"][:, :, 3:4], op0=ALU.mult,
                   op1=ALU.subtract) if v else None)
        s = b.eng if b.on("scalar") else None
        if s:
            b.w("ve")
        b.selfsync("ac", s.activation(out=sb["gst"][:, :, 5:6],
                   in_=sb["gst"][:, :, 4:5], func=AF.Ln,
                   bias=sb["eps"][:, 0:1]) if s else None)
        b.selfsync("ac", s.activation(out=sb["gst"][:, :, 5:6],
                   in_=sb["gst"][:, :, 5:6], func=AF.Exp, scale=-0.5) if s else None)
        ins = None
        if b.on("vector"):
            b.w("ac")
            v = b.eng
            gt = sb["gtile"][bf][:, :].rearrange("p (h e) -> p h e", e=64)
            v.tensor_tensor(out=gt, in0=gt,
                            in1=sb["gst"][:, :, 2:3].broadcast_to([128, 64, 64]),
                            op=ALU.subtract)
            v.tensor_tensor(out=gt, in0=gt,
                            in1=sb["gst"][:, :, 5:6].broadcast_to([128, 64, 64]),
                            op=ALU.mult)
            v.tensor_tensor(out=sb["gtile"][bf][:, :], in0=sb["gtile"][bf][:, :],
                            in1=sb["gnwt"][:, :], op=ALU.mult)
            ins = v.tensor_tensor(out=sb["gtile"][bf][:, :], in0=sb["gtile"][bf][:, :],
                                  in1=sb["gnbt"][:, :], op=ALU.add)
        b.tick("ve", ins)
        b.dma("sc", io["gn_scr"][128 * tc:128 * tc + 128, :], sb["gtile"][bf][:, :],
              wait=["ve"])


def ph_gate(b):
    io, sb = b.io, b.sb
    b.phase()
    for cc in range(NCH):
        bf = cc % 2
        b.dma("sp", sb["kf"][:, :], io["g_scr"][128 * cc:128 * cc + 128, :],
              wait=["ve", "ac", "te"])
        b.dmat("sc", sb["xnf"][bf][:, 0:1024],
               io["gn_scr"].rearrange("t (c p) -> t c p", p=128)[:, cc, :],
               wait=["ve", "sc"])
        ins = None
        if b.on("scalar"):
            b.w("sp")
            b.w("ve")
            ins = b.eng.activation(out=sb["rf"][:, :], in_=sb["kf"][:, :], func=AF.Silu)
        b.tick("ac", ins)
        ins = None
        if b.on("vector"):
            if cc == 0:
                b.w("te")
            b.w("sp")
            b.w("sc")
            b.w("ac")
            ins = b.eng.tensor_tensor(out=sb["mix"][:, cc, :], in0=sb["rf"][:, :],
                                      in1=sb["xnf"][bf][:, 0:1024], op=ALU.mult)
        b.tick("ve", ins)


def ph_gateup(b):
    io, sb = b.io, b.sb
    b.phase()
    for cc in range(NCH):
        b.dmat("sc", sb["mix"][:, cc, :],
               io["xn_tok"].rearrange("t (c p) -> t c p", p=128)[1:1025, cc, :],
               wait=["te", "sc"])
    wg = 0
    for mi in range(43):
        i0 = 256 * mi
        for kc in range(NCH):
            wb = wg % 4
            q = "sc"
            b.dmat(q, sb["wstr"][wb][:, 0:256],
                   io["wgu_scr"][i0:i0 + 256, 128 * kc:128 * kc + 128],
                   wait=[("te", b.wmk(("guW", wg - 3), "te"))])
            b.dmat(q, sb["wstr"][wb][:, 256:512],
                   io["wgu_scr"][INTER + i0:INTER + i0 + 256, 128 * kc:128 * kc + 128])
            ins = None
            if b.on("tensor"):
                if kc == 0:
                    b.w("ac", b.wmk(("guev", mi - 1), "ac"))
                    b.w("ve", b.wmk(("guev2", mi - 1), "ve"))
                b.w(q)
                t = b.eng
                for gi in range(2):
                    for ic in range(2):
                        for th in range(2):
                            ins = t.matmul(
                                b.ps[4 * gi + 2 * ic + th][:, :],
                                sb["wstr"][wb][:, 256 * gi + 128 * ic:
                                               256 * gi + 128 * ic + 128],
                                sb["mix"][:, kc, 512 * th:512 * th + 512],
                                start=(kc == 0), stop=(kc == NCH - 1))
            b.tick("te", ins)
            b.mark(("guW", wg), "te")
            wg += 1
        ins = None
        if b.on("scalar"):
            b.w("te")
            b.w("sc", b.wmk(("guhs", mi - 1), "sc"))
            for k in range(4):
                ins = b.eng.activation(out=sb["ev"][k][:, :], in_=b.ps[k][:, :],
                                       func=AF.Silu)
        b.tick("ac", ins)
        b.mark(("guev", mi), "ac")
        ins = None
        if b.on("vector"):
            b.w("ac")
            b.w("sc", b.wmk(("guhs", mi - 1), "sc"))
            for k in range(4):
                ins = b.eng.tensor_tensor(out=sb["ev2"][k][:, :], in0=sb["ev"][k][:, :],
                                          in1=b.ps[4 + k][:, :], op=ALU.mult)
        b.tick("ve", ins)
        b.mark(("guev2", mi), "ve")
        for ic in range(2):
            for th in range(2):
                b.dma("sc", io["h_scr"][i0 + 128 * ic:i0 + 128 * ic + 128,
                                        512 * th:512 * th + 512],
                      sb["ev2"][2 * ic + th][:, :], wait=["ve"])
        b.mark(("guhs", mi), "sc")


def ph_down(b):
    io, sb = b.io, b.sb
    b.phase()
    wg = 0
    for th in range(2):
        for ic in range(NG_DN):
            b.dma("sp", sb["hbuf"][:, ic, :],
                  io["h_scr"][128 * ic:128 * ic + 128, 512 * th:512 * th + 512],
                  wait=["sc", "te", ("te", b.wmk(("dnh", th - 1), "te"))])
        for hm in range(8):
            pbs = [4 * (hm % 2) + i for i in range(4)]
            for ic in range(NG_DN):
                wb = wg % 4
                q = "sc"
                b.dmat(q, sb["wstr"][wb][:, :],
                       io["wdn_scr"][512 * hm:512 * hm + 512, 128 * ic:128 * ic + 128],
                       wait=[("te", b.wmk(("dnW", wg - 3), "te"))])
                ins = None
                if b.on("tensor"):
                    if ic == 0:
                        b.w("sp")
                        b.w("ac", b.wmk(("dnev", th, hm - 2), "ac"))
                    b.w(q)
                    t = b.eng
                    for hc in range(4):
                        ins = t.matmul(b.ps[pbs[hc]][:, :],
                                       sb["wstr"][wb][:, 128 * hc:128 * hc + 128],
                                       sb["hbuf"][:, ic, :],
                                       start=(ic == 0), stop=(ic == NG_DN - 1))
                b.tick("te", ins)
                b.mark(("dnW", wg), "te")
                wg += 1
            ins = None
            if b.on("scalar"):
                b.w("te")
                b.w("sc", b.wmk(("dnst", th, hm - 1), "sc"))
                for hc in range(4):
                    ins = b.eng.activation(out=sb["ev"][hc][:, :],
                                           in_=b.ps[pbs[hc]][:, :], func=AF.Identity)
            b.tick("ac", ins)
            b.mark(("dnev", th, hm), "ac")
            for hc in range(4):
                n0 = 512 * hm + 128 * hc
                b.dma("sc", io["ffn_scr"][n0:n0 + 128, 512 * th:512 * th + 512],
                      sb["ev"][hc][:, :], wait=["ac"])
            b.mark(("dnst", th, hm), "sc")
        b.mark(("dnh", th), "te")


def ph_final(b):
    io, sb = b.io, b.sb
    b.phase()
    for tc in range(NCK):
        bf = tc % 2
        b.dma("sp", sb["xt"][bf][:, :], io["x"][1 + 128 * tc:129 + 128 * tc, :],
              wait=["ve", "te"])
        b.dmat("sc", sb["gtile"][bf][:, :], io["ffn_scr"][:, 128 * tc:128 * tc + 128],
               wait=["ac", "sc"])
        b.dmat("sc", sb["stok"][:, :], io["s_scr"][:, 128 * tc:128 * tc + 128],
               wait=["ve", "sc"])
        ins = None
        if b.on("scalar"):
            b.w("sc")
            ins = b.eng.activation(out=sb["gsq"][:, :], in_=sb["gtile"][bf][:, :],
                                   func=AF.Silu)
        b.tick("ac", ins)
        ins = None
        if b.on("vector"):
            b.w("sp")
            b.w("sc")
            b.w("ac")
            v = b.eng
            v.tensor_tensor(out=sb["gsq"][:, :], in0=sb["gsq"][:, :],
                            in1=sb["stok"][:, :], op=ALU.mult)
            ins = v.tensor_tensor(out=sb["xt"][bf][:, :], in0=sb["xt"][bf][:, :],
                                  in1=sb["gsq"][:, :], op=ALU.add)
        b.tick("ve", ins)
        b.dma("sp", io["out"][128 * tc:128 * tc + 128, :], sb["xt"][bf][:, :],
              wait=["ve"])


_NC_CACHE = None


def kernel(L0_attn_hidden, L0_attn_w_key, L0_attn_w_value, L0_attn_w_receptance,
           L0_attn_w_gate, L0_attn_w_output, L0_attn_time_mix_key,
           L0_attn_time_mix_value, L0_attn_time_mix_receptance, L0_attn_time_mix_gate,
           L0_attn_time_decay, L0_attn_time_faaaa, L0_attn_gn_weight, L0_attn_gn_bias,
           L0_ffn_qweight_gu, L0_ffn_qzeros_gu, L0_ffn_scales_gu,
           L0_ffn_qweight_dn, L0_ffn_qzeros_dn, L0_ffn_scales_dn):
    global _NC_CACHE
    if _NC_CACHE is None:
        _NC_CACHE = build_nc()
    nc = _NC_CACHE

    x = np.asarray(L0_attn_hidden)
    ct = np.ascontiguousarray
    shared = dict(
        wkT=ct(np.asarray(L0_attn_w_key).T),
        wvT=ct(np.asarray(L0_attn_w_value).T),
        wrT=ct(np.asarray(L0_attn_w_receptance).T),
        wgT=ct(np.asarray(L0_attn_w_gate).T),
        woT=ct(np.asarray(L0_attn_w_output).T),
        tmix=ct(np.stack([np.asarray(L0_attn_time_mix_key),
                          np.asarray(L0_attn_time_mix_value),
                          np.asarray(L0_attn_time_mix_receptance),
                          np.asarray(L0_attn_time_mix_gate)]
                         ).astype(np.float32).reshape(4, NCH, 128).transpose(2, 0, 1)),
        td=ct(np.asarray(L0_attn_time_decay).reshape(NCH, 128).T),
        tf=ct(np.asarray(L0_attn_time_faaaa).reshape(NCH, 128).T),
        gnw=ct(np.asarray(L0_attn_gn_weight)),
        gnb=ct(np.asarray(L0_attn_gn_bias)),
        qguT=ct(np.asarray(L0_ffn_qweight_gu).T),
        zguT=ct(np.asarray(L0_ffn_qzeros_gu).T),
        sguT=ct(np.asarray(L0_ffn_scales_gu).T),
        qdnT=ct(np.asarray(L0_ffn_qweight_dn).T),
        zdnT=ct(np.asarray(L0_ffn_qzeros_dn).T),
        sdnT=ct(np.asarray(L0_ffn_scales_dn).T),
    )

    in_maps = []
    for c in range(8):
        seq, m = c // 4, c % 4
        xs = np.zeros((T + 1, HID), np.float32)
        if m > 0:
            xs[0] = x[seq, m * T - 1]
        xs[1:] = x[seq, m * T:(m + 1) * T]
        qexp = np.full((8, 128), -1e9, np.float32)
        base = 4 * seq
        for mp in range(m):
            qexp[base + mp, :] = -1024.0 * (m - 1 - mp)
        im = dict(shared)
        im["x"] = xs
        im["qexp"] = qexp
        in_maps.append(im)

    res = run_bass_kernel_spmd(nc, in_maps, core_ids=list(range(8)))
    out = np.empty((2, 4096, HID), np.float32)
    for c in range(8):
        seq, m = c // 4, c % 4
        out[seq, m * T:(m + 1) * T] = res.results[c]["out"]
    return out
